# version 02
# baseline (speedup 1.0000x reference)
import numpy as np
import ml_dtypes
from contextlib import ExitStack
import concourse.bass as bass
import concourse.mybir as mybir
from concourse.bass_utils import run_bass_kernel_spmd

F32 = mybir.dt.float32
BF16 = mybir.dt.bfloat16
AF = mybir.ActivationFunctionType
ALU = mybir.AluOpType
AX = mybir.AxisListType

P = 128
D = 1024
T = 4096
C = 512
NT = T // C
KC = 8
JC = 32
INNER = 4096
MGK = float(3 * 2 ** 22)
EPS = 1e-5


class Res:
    __slots__ = ("w", "r")

    def __init__(self):
        self.w = None
        self.r = []


class Sched:
    ORDER = ("sync", "tensor", "vector", "scalar", "gpsimd")
    ESEM = dict(sync=None, tensor="t", vector="v", scalar="s", gpsimd="g")

    def __init__(self):
        self.steps = {e: [] for e in self.ORDER}
        self.cnt = dict(sy=0, t=0, v=0, s=0, g=0)

    def add(self, eng, fn, reads=(), writes=(), deps=(), strict=False, dma=False):
        sem = "sy" if dma else self.ESEM[eng]
        amt = 16 if dma else 1
        dl = [d for d in deps if d is not None]
        for r in reads:
            if r.w is not None:
                dl.append(r.w)
        for r in writes:
            if r.w is not None:
                dl.append(r.w)
            dl.extend(r.r)
        self.cnt[sem] += amt
        tok = (sem, self.cnt[sem])
        self.steps[eng].append((fn, dl, strict, tok))
        for r in reads:
            r.r.append(tok)
        for r in writes:
            r.w = tok
            r.r = []
        return tok

    def waitonly(self, eng, deps):
        self.steps[eng].append((None, [d for d in deps if d is not None], True, None))

    def barrier_toks(self):
        return [(s, v) for s, v in self.cnt.items() if v > 0]

    def emit(self, eng_name, eng, sems):
        own = self.ESEM[eng_name]
        maxw = {}
        for fn, deps, strict, tok in self.steps[eng_name]:
            for dsem, dval in sorted(set(deps)):
                if dsem == own and not strict:
                    continue
                if maxw.get(dsem, 0) >= dval:
                    continue
                eng.wait_ge(sems[dsem], dval)
                maxw[dsem] = dval
            if fn is None:
                continue
            inst = fn(eng)
            if isinstance(inst, (list, tuple)):
                inst = inst[-1]
            inst.then_inc(sems[tok[0]], 16 if tok[0] == "sy" else 1)


def build_nc(stage_a_only=False):
    nc = bass.Bass("TRN2", target_bir_lowering=False)
    dt = nc.dram_tensor
    x_d = dt("x", [D, T], F32, kind="ExternalInput").ap()
    w_in = [dt(n, [D, D], F32, kind="ExternalInput").ap() for n in ("wi", "wf", "wg", "wo")]
    ci_d = dt("ci", [D, 4], F32, kind="ExternalInput").ap()
    cf_d = dt("cf", [D, 4], F32, kind="ExternalInput").ap()
    gn_d = dt("gn", [D, 1], F32, kind="ExternalInput").ap()
    wff_d = {"g": dt("wgt", [D, INNER], F32, kind="ExternalInput").ap(),
             "u": dt("wut", [D, INNER], F32, kind="ExternalInput").ap(),
             "d": dt("wdt", [INNER, D], F32, kind="ExternalInput").ap()}
    id_d = dt("ident", [P, P], F32, kind="ExternalInput").ap()
    cst_d = dt("consts", [P, 4], F32, kind="ExternalInput").ap()
    out_d = dt("out", [D, T], F32, kind="ExternalOutput").ap()

    zx1 = dt("zx1", [KC, P, T], F32, kind="Internal").ap()
    zwa = dt("zwa", [4, KC, P, D], BF16, kind="Internal").ap()
    zgu = dt("zgu", [JC, P, 2, KC, P], BF16, kind="Internal").ap()
    zwd = dt("zwd", [JC, P, D], BF16, kind="Internal").ap()

    S = Sched()

    with ExitStack() as stack:
        GIANT = stack.enter_context(nc.sbuf_tensor("giant", [P, 49200], F32))
        G = GIANT.ap()
        ps = [stack.enter_context(nc.psum_tensor("ps%d" % i, [P, C], F32)) for i in range(8)]
        sems = {s: stack.enter_context(nc.semaphore("sem_" + s))
                for s in ("sy", "t", "v", "s", "g")}

        def sbt(name, shape, dtp=F32):
            return stack.enter_context(nc.sbuf_tensor(name, shape, dtp))

        tokm = sbt("tokm", [P, 32])
        carry = sbt("carry", [P, KC])
        convw = sbt("convw", [P, KC, 8])
        gncol = sbt("gncol", [P, KC, 1])
        wrs = sbt("wrs", [P, JC, 1])
        ocol_bf = sbt("ocol_bf", [P, 1], BF16)
        orow_bf = sbt("orow_bf", [1, P], BF16)
        orow_f = sbt("orow_f", [1, P])
        omat_f = sbt("omat_f", [P, P])
        id_sb = sbt("id_sb", [P, P])
        pcol = sbt("pcol", [P, 4, 1])
        prcp = sbt("prcp", [P, 1, 1])
        rcpb = sbt("rcpb", [P, 2, C])
        cst = sbt("cst", [P, 4])

        A3 = G[:, 0:4120].rearrange("p (c n) -> p c n", c=KC)
        B3 = G[:, 4120:8240].rearrange("p (c n) -> p c n", c=KC)
        Dd = G[:, 8240:12336]
        Dd3 = Dd.rearrange("p (c n) -> p c n", c=KC)
        Ff = G[:, 12336:16432]
        Ff3 = Ff.rearrange("p (c n) -> p c n", c=KC)
        Xv3 = [G[:, 16432 + b * 4096: 16432 + (b + 1) * 4096]
               .rearrange("p (c n) -> p c n", c=KC) for b in range(2)]
        sqbf3 = G[:, 24624:26672].bitcast(BF16).rearrange("p (c n) -> p c n", c=KC)
        sgx3 = G[:, 26672:28720].bitcast(BF16).rearrange("p (c n) -> p c n", c=KC)
        lnx3 = G[:, 28720:30768].bitcast(BF16).rearrange("p (c n) -> p c n", c=KC)
        tmpq = G[:, 28720:29232]
        haccv = G[:, 29232:29744]
        tmpbf = G[:, 29744:30256].bitcast(BF16).rearrange("p (u n) -> p u n", u=2)
        o2bf3 = G[:, 30768:32816].bitcast(BF16).rearrange("p (c n) -> p c n", c=KC)
        bc = G[:, 32816:34864].rearrange("p (c n) -> p c n", c=4)
        wsl = [G[:, 34864 + b * 4096: 34864 + (b + 1) * 4096].bitcast(BF16)
               .rearrange("p (c d) -> p c d", c=KC) for b in range(2)]
        guv = [G[:, 34864 + r * 1024: 34864 + (r + 1) * 1024].bitcast(BF16)
               .rearrange("p (u k o) -> p u k o", u=2, k=KC) for r in range(4)]
        dnv = [G[:, 38960 + r * 512: 38960 + (r + 1) * 512].bitcast(BF16) for r in range(4)]
        hbf = G[:, 0:8192].bitcast(BF16).rearrange("p (j n) -> p j n", j=JC)
        wraw = G[:, 0:32768]
        wqbf = G[:, 32816:41008].bitcast(BF16)
        rowm1 = G[0:1, 43056:45104].bitcast(BF16)
        rowt = G[0:1, 45104:47152].rearrange("p (c n) -> p c n", c=4)
        rowbf = G[0:1, 47152:47664].bitcast(BF16).rearrange("p (c n) -> p c n", c=2)
        cwbf = G[0:1, 47664:49200].bitcast(BF16).rearrange("p (c n) -> p c n", c=3)
        rowr = [rcpb.ap()[0:1, i, :] for i in range(2)]

        def bsp(ap2d, c=KC):
            return ap2d.rearrange("p (o n) -> p o n", o=1).broadcast_to([P, c, C])

        R = lambda: Res()
        rA, rAh, rB, rBh, rD, rF = R(), R(), R(), R(), R(), R()
        rX = [R(), R()]
        rSQ, rSG, rLNX, rO2 = R(), R(), R(), R()
        rTQ, rHA = R(), R()
        rTB = [R(), R()]
        rBC = [R() for _ in range(4)]
        rBK = [R() for _ in range(8)]
        rROW = [R() for _ in range(4)]
        rRBF = [R(), R()]
        rM1 = [R() for _ in range(NT)]
        rTK = [R() for _ in range(8)]
        rCAR = [R() for _ in range(KC)]
        rWSL = [R(), R()]
        rGUV = [R() for _ in range(4)]
        rDNV = [R() for _ in range(4)]
        rH = [R() for _ in range(JC)]
        rZX1 = [R() for _ in range(NT)]
        rZWA = [R() for _ in range(4)]
        rZGU = [R() for _ in range(JC)]
        rZWD = [R() for _ in range(JC)]
        rWRAW, rWQ = R(), R()
        rPC, rWRS, rCW, rGN = R(), R(), R(), R()
        rID, rONES, rCST, rCWB = R(), R(), R(), R()
        rRCP = [R(), R()]

        S.add("vector", lambda e: e.memset(ocol_bf[:], 1.0), writes=[rONES])
        S.add("vector", lambda e: e.memset(orow_bf[:], 1.0), writes=[rONES])
        S.add("vector", lambda e: e.memset(orow_f[:], 1.0), writes=[rONES])
        S.add("vector", lambda e: e.memset(omat_f[:], 1.0), writes=[rONES])
        S.add("vector", lambda e: e.memset(carry[:], 0.0), writes=rCAR)
        S.add("sync", lambda e: e.dma_start(
            out=convw.ap()[:, :, 0:4], in_=ci_d.rearrange("(c p) k -> p c k", p=P)),
            writes=[rCW], dma=True)
        S.add("sync", lambda e: e.dma_start(
            out=convw.ap()[:, :, 4:8], in_=cf_d.rearrange("(c p) k -> p c k", p=P)),
            writes=[rCW], dma=True)
        S.add("sync", lambda e: e.dma_start(
            out=gncol[:], in_=gn_d.rearrange("(c p) o -> p c o", p=P),
            allow_slow_non_contiguous=True), writes=[rGN], dma=True)
        S.add("sync", lambda e: e.dma_start(out=id_sb[:], in_=id_d), writes=[rID], dma=True)
        S.add("sync", lambda e: e.dma_start(out=cst[:], in_=cst_d), writes=[rCST], dma=True)
        c_eps = cst.ap()[0:1, 0:1]
        c_1oD = cst.ap()[0:1, 1:2]
        c_neg1 = cst.ap()[:, 2:3]
        c_neg1r = cst.ap()[0:1, 2:3]

        wv8 = wraw[:, 0:8192].rearrange("p (c d) -> p c d", c=KC)
        wq8 = wqbf[:, 0:8192].rearrange("p (c d) -> p c d", c=KC)
        for w in range(4):
            S.add("sync", lambda e, s=w_in[w]: e.dma_start(
                out=wv8, in_=s.rearrange("(c p) d -> p c d", p=P)),
                writes=[rWRAW], dma=True)
            if w < 3:
                S.add("scalar", lambda e: e.activation(
                    out=wq8, in_=wv8, func=AF.Copy), reads=[rWRAW], writes=[rWQ])
                for h in range(2):
                    def fcs(e, h=h):
                        return [e.matmul(ps[4 + h][0:1, :], ocol_bf[:],
                                         wq8[:, c, h * C:(h + 1) * C],
                                         start=(c == 0), stop=(c == KC - 1))
                                for c in range(KC)]
                    S.add("tensor", fcs, reads=[rWQ, rONES], writes=[rBK[4 + h]])
                    S.add("scalar", lambda e, h=h, w=w: e.activation(
                        out=cwbf[:, w, h * C:(h + 1) * C], in_=ps[4 + h][0:1, :],
                        func=AF.Copy), reads=[rBK[4 + h]], writes=[rCWB])
            else:
                for c in range(KC):
                    S.add("vector", lambda e, c=c: e.tensor_scalar(
                        out=wv8[:, c, :], in0=wv8[:, c, :],
                        scalar1=gncol[:, c, :], scalar2=None, op0=ALU.mult),
                        reads=[rGN], writes=[rWRAW], strict=True)
                S.add("vector", lambda e: e.tensor_reduce(
                    out=wrs[:, 0:KC, :], in_=wv8, axis=AX.X, op=ALU.add),
                    reads=[rWRAW], writes=[rWRS])
                S.add("vector", lambda e: e.tensor_scalar(
                    out=wrs[:, 0:KC, :], in0=wrs[:, 0:KC, :],
                    scalar1=-1.0 / D, scalar2=None, op0=ALU.mult), writes=[rWRS])
                for c in range(KC):
                    S.add("vector", lambda e, c=c: e.tensor_scalar(
                        out=wq8[:, c, :], in0=wv8[:, c, :],
                        scalar1=wrs[:, c, :], scalar2=None, op0=ALU.add),
                        reads=[rWRAW, rWRS], writes=[rWQ], strict=True)
            S.add("sync", lambda e, w=w: e.dma_start(
                out=zwa[w].rearrange("c p d -> p c d"), in_=wq8),
                reads=[rWQ], writes=[rZWA[w]], dma=True)

        if not stage_a_only:
            for nm in ("g", "u", "d"):
                nch = JC if nm == "d" else KC
                wv = wraw.rearrange("p (c n) -> p c n", c=nch)
                S.add("sync", lambda e, s=wff_d[nm], wv=wv: e.dma_start(
                    out=wv, in_=s.rearrange("(c p) n -> p c n", p=P)),
                    writes=[rWRAW], dma=True)
                S.add("vector", lambda e: e.tensor_reduce(
                    out=pcol[:, 0:1, :], in_=wraw.rearrange("p (o n) -> p o n", o=1),
                    axis=AX.X, op=ALU.add, apply_absolute_value=True),
                    reads=[rWRAW], writes=[rPC])
                S.add("tensor", lambda e: e.matmul(
                    ps[0][:, 0:1], omat_f[:], pcol[:, 0, :], start=True, stop=True),
                    reads=[rPC, rONES], writes=[rBK[0]])
                S.add("vector", lambda e: e.tensor_scalar(
                    out=pcol[:, 1, :], in0=ps[0][:, 0:1], scalar1=1.0 / (D * INNER),
                    scalar2=1e-5, op0=ALU.mult, op1=ALU.max),
                    reads=[rBK[0]], writes=[rPC])
                S.add("vector", lambda e: e.reciprocal(
                    out=prcp[:, 0, :], in_=pcol[:, 1, :]), reads=[rPC], writes=[rPC],
                    strict=True)
                S.add("vector", lambda e: e.tensor_scalar(
                    out=wraw[:, :], in0=wraw[:, :], scalar1=prcp[:, 0, :],
                    scalar2=1.0, op0=ALU.mult, op1=ALU.min),
                    reads=[rPC], writes=[rWRAW], strict=True)
                S.add("vector", lambda e: e.tensor_scalar(
                    out=wraw[:, :], in0=wraw[:, :], scalar1=-1.0,
                    scalar2=MGK, op0=ALU.max, op1=ALU.add), writes=[rWRAW])
                for hh in range(2):
                    lo, hi = hh * 16384, (hh + 1) * 16384
                    S.add("vector", lambda e, lo=lo, hi=hi: e.tensor_scalar(
                        out=wqbf[:, :], in0=wraw[:, lo:hi], scalar1=MGK,
                        scalar2=pcol[:, 1, :], op0=ALU.subtract, op1=ALU.mult),
                        reads=[rWRAW, rPC], writes=[rWQ], strict=True)
                    if nm == "d":
                        jh = nch // 2
                        wqh = wqbf.rearrange("p (c n) -> p c n", c=jh)
                        S.add("vector", lambda e, wqh=wqh, hh=hh, jh=jh: e.tensor_reduce(
                            out=wrs[:, hh * jh:(hh + 1) * jh, :], in_=wqh,
                            axis=AX.X, op=ALU.add), reads=[rWQ], writes=[rWRS])
                        S.add("vector", lambda e, hh=hh, jh=jh: e.tensor_scalar(
                            out=wrs[:, hh * jh:(hh + 1) * jh, :],
                            in0=wrs[:, hh * jh:(hh + 1) * jh, :],
                            scalar1=-1.0 / D, scalar2=None, op0=ALU.mult), writes=[rWRS])
                        for jj in range(jh):
                            j = hh * jh + jj
                            S.add("vector", lambda e, jj=jj, j=j, wqh=wqh: e.tensor_scalar(
                                out=wqh[:, jj, :], in0=wqh[:, jj, :], scalar1=wrs[:, j, :],
                                scalar2=None, op0=ALU.add),
                                reads=[rWRS], writes=[rWQ], strict=True)
                            S.add("sync", lambda e, jj=jj, j=j, wqh=wqh: e.dma_start(
                                out=zwd[j], in_=wqh[:, jj, :]),
                                reads=[rWQ], writes=[rZWD[j]], dma=True)
                    else:
                        gi = 0 if nm == "g" else 1
                        wq4 = wqbf.rearrange("p (c n) -> p c n", c=4)
                        for j in range(JC):
                            S.add("sync", lambda e, j=j, gi=gi, hh=hh, wq4=wq4: e.dma_start(
                                out=zgu[j][:, gi, 4 * hh:4 * (hh + 1), :],
                                in_=wq4[:, :, j * P:(j + 1) * P]),
                                reads=[rWQ], writes=[rZGU[j]], dma=True)

        prep_done = S.barrier_toks()

        S.add("vector", lambda e: e.memset(A3[:, :, 0:3], 0.0),
              writes=[rAh], deps=prep_done)
        S.add("vector", lambda e: e.memset(B3[:, :, 0:3], 0.0),
              writes=[rBh], deps=prep_done)

        def wslot_load(k, w):
            sl = (4 * k + w) % 2
            tok = S.add("sync", lambda e, w=w, sl=sl: e.dma_start(
                out=wsl[sl], in_=zwa[w].rearrange("c p d -> p c d")),
                reads=[rZWA[w]], writes=[rWSL[sl]],
                deps=prep_done if (k == 0 and w < 2) else [], dma=True)
            return sl, tok

        def stat_mms(psi, rhs3, extra=()):
            def fn(e):
                return [e.matmul(ps[psi][0:1, :], ocol_bf[:], rhs3[:, c, :],
                                 start=(c == 0), stop=(c == KC - 1)) for c in range(KC)]
            return S.add("tensor", fn, reads=[rONES] + list(extra), writes=[rBK[psi]])

        def xload(k):
            b = k % 2
            t0 = k * C
            return S.add("sync", lambda e, b=b, t0=t0: e.dma_start(
                out=Xv3[b], in_=x_d.rearrange("(c p) t -> p c t", p=P)[:, :, t0:t0 + C]),
                writes=[rX[b]], deps=prep_done if k == 0 else [], dma=True)

        def ln1_prefix(k):
            b = k % 2
            t0 = k * C
            Xk3 = Xv3[b]
            m1row = rowm1[:, t0:t0 + C]
            S.add("scalar", lambda e, Xk3=Xk3: e.activation(
                out=lnx3, in_=Xk3, func=AF.Copy), reads=[rX[b]], writes=[rLNX])
            S.add("scalar", lambda e, Xk3=Xk3: e.activation(
                out=sqbf3, in_=Xk3, func=AF.Square), reads=[rX[b]], writes=[rSQ])
            stat_mms(4, lnx3, extra=[rLNX])
            stat_mms(5, sqbf3, extra=[rSQ])
            S.add("vector", lambda e, m1row=m1row: e.tensor_scalar(
                out=m1row, in0=ps[4][0:1, :], scalar1=1.0 / D, scalar2=None,
                op0=ALU.mult), reads=[rBK[4]], writes=[rM1[k]])
            S.add("scalar", lambda e: e.activation(
                out=rowt[:, 0, :], in_=ps[5][0:1, :], func=AF.Copy, scale=c_1oD),
                reads=[rBK[5], rCST], writes=[rROW[0]], strict=True)
            S.add("vector", lambda e, m1row=m1row: e.tensor_mul(
                out=rowt[:, 1, :], in0=m1row, in1=m1row),
                reads=[rM1[k]], writes=[rROW[1]])
            S.add("vector", lambda e: e.tensor_sub(
                out=rowt[:, 1, :], in0=rowt[:, 0, :], in1=rowt[:, 1, :]),
                reads=[rROW[0]], writes=[rROW[1]])
            S.add("scalar", lambda e: e.activation(
                out=rowt[:, 0, :], in_=rowt[:, 1, :], func=AF.Sqrt, bias=c_eps),
                reads=[rROW[1], rCST], writes=[rROW[0]], strict=True)
            S.add("vector", lambda e: e.reciprocal(
                out=rowr[0], in_=rowt[:, 0, :]), reads=[rROW[0]],
                writes=[rROW[2], rRCP[0]])
            S.add("vector", lambda e, m1row=m1row: e.tensor_mul(
                out=rowt[:, 2, :], in0=m1row, in1=rowr[0]),
                reads=[rM1[k], rROW[2]], writes=[rROW[3]])
            S.add("scalar", lambda e: e.activation(
                out=rowbf[:, 1, :], in_=rowt[:, 2, :], func=AF.Copy, scale=c_neg1r),
                reads=[rROW[3], rCST], writes=[rRBF[1]], strict=True)
            S.add("scalar", lambda e: e.activation(
                out=rowbf[:, 0, :], in_=rowr[0], func=AF.Copy),
                reads=[rROW[2]], writes=[rRBF[0]])
            S.add("tensor", lambda e: e.matmul(
                ps[6][:], orow_bf[:], rowbf[:, 0, :], start=True, stop=True),
                reads=[rRBF[0], rONES], writes=[rBK[6]])
            S.add("scalar", lambda e: e.activation(
                out=bc[:, 0, :], in_=ps[6][:], func=AF.Copy),
                reads=[rBK[6]], writes=[rBC[0]])
            S.add("vector", lambda e, Xk3=Xk3, b=b: e.tensor_mul(
                out=lnx3, in0=Xk3, in1=bsp(bc[:, 0, :])),
                reads=[rX[b], rBC[0]], writes=[rLNX])

        xload(0)
        xload(1)
        wq_pend = {}
        for w in range(2):
            wq_pend[(0, w)] = wslot_load(0, w)
        ln1_prefix(0)

        tk_x1st = [None] * NT
        for k in range(NT):
            b = k % 2
            t0 = k * C
            Xk3 = Xv3[b]
            rXk = rX[b]

            for w, tgt in ((0, "i"), (1, "f"), (2, "g")):
                sl, tok_w = wq_pend.pop((k, w))
                for oc in range(KC):
                    bk = oc % 4
                    def fn(e, sl=sl, oc=oc, w=w):
                        ins = [e.matmul(ps[oc % 4][:], wsl[sl][:, c, oc * P:(oc + 1) * P],
                                        lnx3[:, c, :], start=(c == 0), stop=False)
                               for c in range(KC)]
                        ins.append(e.matmul(ps[oc % 4][:],
                                            cwbf[:, w, oc * P:(oc + 1) * P],
                                            rowbf[:, 1, :], start=False, stop=True))
                        return ins
                    S.add("tensor", fn, reads=[rWSL[sl], rLNX, rCWB, rRBF[1]],
                          writes=[rBK[bk]], deps=[tok_w])
                    if tgt == "i":
                        S.add("scalar", lambda e, oc=oc, bk=bk: e.activation(
                            out=A3[:, oc, 3:3 + C], in_=ps[bk][:], func=AF.Copy),
                            reads=[rBK[bk]], writes=[rA])
                    elif tgt == "f":
                        S.add("scalar", lambda e, oc=oc, bk=bk: e.activation(
                            out=B3[:, oc, 3:3 + C], in_=ps[bk][:], func=AF.Copy),
                            reads=[rBK[bk]], writes=[rB])
                    else:
                        S.add("scalar", lambda e, oc=oc, bk=bk: e.activation(
                            out=sgx3[:, oc, :], in_=ps[bk][:], func=AF.Silu),
                            reads=[rBK[bk]], writes=[rSG])
                nxt = (k, w + 2) if w < 2 else (k + 1, 0)
                if nxt[0] < NT and nxt not in wq_pend:
                    wq_pend[nxt] = wslot_load(*nxt)

            if k + 1 < NT:
                ln1_prefix(k + 1)

            for c in range(KC):
                S.add("vector", lambda e, c=c: e.tensor_scalar(
                    out=Dd3[:, c, :], in0=A3[:, c, 3:3 + C],
                    scalar1=convw[:, c, 3:4], scalar2=None, op0=ALU.mult),
                    reads=[rA, rCW], writes=[rD], strict=True)
                for kk in (2, 1, 0):
                    S.add("vector", lambda e, c=c, kk=kk: e.scalar_tensor_tensor(
                        out=Dd3[:, c, :], in0=A3[:, c, kk:kk + C],
                        scalar=convw[:, c, kk:kk + 1], in1=Dd3[:, c, :],
                        op0=ALU.mult, op1=ALU.add),
                        reads=[rA, rAh, rCW], writes=[rD], strict=True)
            for c in range(KC):
                S.add("vector", lambda e, c=c: e.tensor_scalar(
                    out=Ff3[:, c, :], in0=B3[:, c, 3:3 + C],
                    scalar1=convw[:, c, 7:8], scalar2=None, op0=ALU.mult),
                    reads=[rB, rCW], writes=[rF], strict=True)
                for kk in (2, 1, 0):
                    S.add("vector", lambda e, c=c, kk=kk: e.scalar_tensor_tensor(
                        out=Ff3[:, c, :], in0=B3[:, c, kk:kk + C],
                        scalar=convw[:, c, kk + 4:kk + 5], in1=Ff3[:, c, :],
                        op0=ALU.mult, op1=ALU.add),
                        reads=[rB, rBh, rCW], writes=[rF], strict=True)
            S.add("vector", lambda e: e.tensor_copy(
                out=A3[:, :, 0:3], in_=A3[:, :, C:C + 3]), reads=[rA], writes=[rAh])
            S.add("gpsimd", lambda e: e.tensor_copy(
                out=B3[:, :, 0:3], in_=B3[:, :, C:C + 3]), reads=[rB], writes=[rBh])
            S.add("scalar", lambda e: e.activation(
                out=A3[:, :, 3:3 + C], in_=Dd3, func=AF.Silu),
                reads=[rD], writes=[rA])
            S.add("scalar", lambda e: e.activation(
                out=B3[:, :, 3:3 + C], in_=Ff3, func=AF.Sigmoid, scale=c_neg1),
                reads=[rF, rCST], writes=[rB], strict=True)
            S.add("scalar", lambda e: e.activation(
                out=Ff3, in_=Ff3, func=AF.Sigmoid), reads=[rF], writes=[rF])
            S.add("vector", lambda e: e.tensor_mul(
                out=Dd3, in0=A3[:, :, 3:3 + C], in1=B3[:, :, 3:3 + C]),
                reads=[rA, rB], writes=[rD])
            for c in range(KC):
                S.add("vector", lambda e, c=c: e.tensor_tensor_scan(
                    out=Ff3[:, c, :], data0=Ff3[:, c, :], data1=Dd3[:, c, :],
                    initial=carry[:, c:c + 1], op0=ALU.mult, op1=ALU.add),
                    reads=[rD, rCAR[c]], writes=[rF], strict=True)
                S.add("vector", lambda e, c=c: e.tensor_copy(
                    out=carry[:, c:c + 1], in_=Ff3[:, c, C - 1:C]),
                    reads=[rF], writes=[rCAR[c]], strict=True)
            S.add("scalar", lambda e: e.activation(
                out=sqbf3, in_=Ff3, func=AF.Square), reads=[rF], writes=[rSQ])
            stat_mms(4, sqbf3, extra=[rSQ])
            S.add("scalar", lambda e: e.activation(
                out=rowt[:, 3, :], in_=ps[4][0:1, :], func=AF.Sqrt,
                scale=c_1oD, bias=c_eps), reads=[rBK[4], rCST], writes=[rROW[3]],
                strict=True)
            S.add("vector", lambda e: e.reciprocal(
                out=rowr[1], in_=rowt[:, 3, :]), reads=[rROW[3]], writes=[rRCP[1]])
            S.add("scalar", lambda e: e.activation(
                out=rowbf[:, 0, :], in_=rowr[1], func=AF.Copy),
                reads=[rRCP[1], rRBF[0]], writes=[rRBF[0]])
            S.add("tensor", lambda e: e.matmul(
                ps[6][:], orow_bf[:], rowbf[:, 0, :], start=True, stop=True),
                reads=[rRBF[0], rONES], writes=[rBK[6]])
            S.add("scalar", lambda e: e.activation(
                out=bc[:, 1, :], in_=ps[6][:], func=AF.Copy),
                reads=[rBK[6]], writes=[rBC[1]])
            S.add("gpsimd", lambda e: e.tensor_mul(
                out=Dd3, in0=Ff3, in1=sgx3), reads=[rF, rSG], writes=[rD])
            S.add("vector", lambda e: e.tensor_mul(
                out=o2bf3, in0=Dd3, in1=bsp(bc[:, 1, :])),
                reads=[rD, rBC[1]], writes=[rO2])
            sl, tok_w = wq_pend.pop((k, 3))
            for oc in range(KC):
                bk = oc % 4
                def fno(e, sl=sl, oc=oc):
                    return [e.matmul(ps[oc % 4][:], wsl[sl][:, c, oc * P:(oc + 1) * P],
                                     o2bf3[:, c, :], start=(c == 0), stop=(c == KC - 1))
                            for c in range(KC)]
                S.add("tensor", fno, reads=[rWSL[sl], rO2], writes=[rBK[bk]], deps=[tok_w])
                S.add("scalar", lambda e, oc=oc, bk=bk: e.activation(
                    out=Dd3[:, oc, :], in_=ps[bk][:], func=AF.Copy),
                    reads=[rBK[bk]], writes=[rD])
            nxt = (k + 1, 1)
            if nxt[0] < NT:
                wq_pend[nxt] = wslot_load(*nxt)
            S.add("scalar", lambda e: e.activation(
                out=sqbf3, in_=Dd3, func=AF.Square), reads=[rD], writes=[rSQ])
            stat_mms(5, sqbf3, extra=[rSQ])
            S.add("scalar", lambda e: e.activation(
                out=rowt[:, 3, :], in_=ps[5][0:1, :], func=AF.Sqrt,
                scale=c_1oD, bias=c_eps), reads=[rBK[5], rCST], writes=[rROW[3]],
                strict=True)
            S.add("vector", lambda e: e.reciprocal(
                out=rowr[1], in_=rowt[:, 3, :]), reads=[rROW[3]], writes=[rRCP[1]])
            S.add("tensor", lambda e: e.matmul(
                ps[7][:], orow_f[:], rowr[1], start=True, stop=True),
                reads=[rRCP[1], rONES], writes=[rBK[7]])
            S.add("scalar", lambda e: e.activation(
                out=bc[:, 2, :], in_=ps[7][:], func=AF.Copy),
                reads=[rBK[7]], writes=[rBC[2]])
            S.add("gpsimd", lambda e: e.tensor_mul(
                out=Dd3, in0=Dd3, in1=bsp(bc[:, 2, :])), reads=[rBC[2]], writes=[rD])
            S.add("gpsimd", lambda e, Xk3=Xk3: e.tensor_add(
                out=Xk3, in0=Xk3, in1=Dd3), reads=[rD], writes=[rXk])
            if stage_a_only:
                tk_x1st[k] = S.add("sync", lambda e, b=b, t0=t0: e.dma_start(
                    out=out_d.rearrange("(c p) t -> p c t", p=P)[:, :, t0:t0 + C],
                    in_=Xv3[b]), reads=[rXk], dma=True)
            else:
                tk_x1st[k] = S.add("sync", lambda e, b=b, t0=t0: e.dma_start(
                    out=zx1.rearrange("c p t -> p c t")[:, :, t0:t0 + C],
                    in_=Xv3[b]), reads=[rXk], writes=[rZX1[k]], dma=True)
            if k + 2 < NT:
                xload(k + 2)

        if stage_a_only:
            S.waitonly("sync", tk_x1st)
            _emit(nc, S, sems)
            return nc

        stage_a_done = S.barrier_toks()

        tk_ox = [None] * NT
        for k in range(NT):
            b = k % 2
            t0 = k * C
            Xk3 = Xv3[b]
            rXk = rX[b]
            m1row = rowm1[:, t0:t0 + C]

            S.add("sync", lambda e, b=b, t0=t0: e.dma_start(
                out=Xv3[b], in_=zx1.rearrange("c p t -> p c t")[:, :, t0:t0 + C]),
                reads=[rZX1[k]], writes=[rXk],
                deps=stage_a_done if k == 0 else [], dma=True)

            S.add("scalar", lambda e, Xk3=Xk3: e.activation(
                out=sqbf3, in_=Xk3, func=AF.Square), reads=[rXk], writes=[rSQ])
            stat_mms(4, sqbf3, extra=[rSQ])
            S.add("scalar", lambda e: e.activation(
                out=rowt[:, 0, :], in_=ps[4][0:1, :], func=AF.Copy, scale=c_1oD),
                reads=[rBK[4], rCST], writes=[rROW[0]], strict=True)
            S.add("vector", lambda e, m1row=m1row: e.tensor_mul(
                out=rowt[:, 1, :], in0=m1row, in1=m1row),
                reads=[rM1[k]], writes=[rROW[1]], deps=stage_a_done if k == 0 else [])
            S.add("vector", lambda e: e.tensor_sub(
                out=rowt[:, 1, :], in0=rowt[:, 0, :], in1=rowt[:, 1, :]),
                reads=[rROW[0]], writes=[rROW[1]])
            S.add("scalar", lambda e: e.activation(
                out=rowt[:, 0, :], in_=rowt[:, 1, :], func=AF.Sqrt, bias=c_eps),
                reads=[rROW[1], rCST], writes=[rROW[0]], strict=True)
            S.add("vector", lambda e: e.reciprocal(
                out=rowr[0], in_=rowt[:, 0, :]), reads=[rROW[0]],
                writes=[rROW[2], rRCP[0]])
            S.add("vector", lambda e, m1row=m1row: e.tensor_mul(
                out=rowt[:, 2, :], in0=m1row, in1=rowr[0]),
                reads=[rM1[k], rROW[2]], writes=[rROW[3]])
            S.add("tensor", lambda e: e.matmul(
                ps[6][:], orow_f[:], rowr[0], start=True, stop=True),
                reads=[rROW[2], rONES], writes=[rBK[6]])
            S.add("scalar", lambda e: e.activation(
                out=bc[:, 0, :], in_=ps[6][:], func=AF.Copy),
                reads=[rBK[6]], writes=[rBC[0]])
            S.add("tensor", lambda e: e.matmul(
                ps[7][:], orow_f[:], rowt[:, 2, :], start=True, stop=True),
                reads=[rROW[3], rONES], writes=[rBK[7]])
            S.add("scalar", lambda e: e.activation(
                out=bc[:, 1, :], in_=ps[7][:], func=AF.Copy),
                reads=[rBK[7]], writes=[rBC[1]])
            S.add("vector", lambda e, Xk3=Xk3: e.tensor_mul(
                out=Ff3, in0=Xk3, in1=bsp(bc[:, 0, :])), reads=[rXk, rBC[0]], writes=[rF])
            S.add("gpsimd", lambda e: e.tensor_sub(
                out=Ff3, in0=Ff3, in1=bsp(bc[:, 1, :])), reads=[rBC[1]], writes=[rF])
            S.add("scalar", lambda e: e.activation(
                out=Dd3, in_=Ff3, func=AF.Square), reads=[rF], writes=[rD])
            S.add("vector", lambda e: e.tensor_tensor(
                out=Dd3[:, 0:4, :], in0=Dd3[:, 0:4, :], in1=Dd3[:, 4:8, :],
                op=ALU.max), writes=[rD])
            S.add("vector", lambda e: e.tensor_tensor(
                out=Dd3[:, 0:2, :], in0=Dd3[:, 0:2, :], in1=Dd3[:, 2:4, :],
                op=ALU.max), writes=[rD])
            S.add("vector", lambda e: e.tensor_tensor(
                out=Dd3[:, 0, :], in0=Dd3[:, 0, :], in1=Dd3[:, 1, :],
                op=ALU.max), writes=[rD])
            S.add("tensor", lambda e: [
                e.transpose(ps[6][:, g * P:(g + 1) * P], Dd3[:, 0, g * P:(g + 1) * P],
                            id_sb[:]) for g in range(4)],
                reads=[rD, rID], writes=[rBK[6]])
            S.add("vector", lambda e: e.tensor_reduce(
                out=tokm.ap()[:, 0:4].rearrange("p (g o) -> p g o", o=1),
                in_=ps[6][:].rearrange("p (g n) -> p g n", g=4),
                axis=AX.X, op=ALU.max), reads=[rBK[6]], writes=[rTK[0]])
            S.add("scalar", lambda e: e.activation(
                out=tokm.ap()[:, 4:8], in_=tokm.ap()[:, 0:4], func=AF.Sqrt),
                reads=[rTK[0]], writes=[rTK[1]])
            S.add("vector", lambda e: e.tensor_scalar(
                out=tokm.ap()[:, 4:8], in0=tokm.ap()[:, 4:8], scalar1=1e-5,
                scalar2=None, op0=ALU.max), writes=[rTK[1]])
            S.add("vector", lambda e: e.tensor_scalar(
                out=tokm.ap()[:, 12:16], in0=tokm.ap()[:, 4:8], scalar1=1.0 / 127.0,
                scalar2=None, op0=ALU.mult), reads=[rTK[1]], writes=[rTK[3]])
            S.add("tensor", lambda e: [
                e.matmul(ps[6][:, g * P:(g + 1) * P],
                         tokm.ap()[:, 4 + g:5 + g].broadcast_to([P, P]), id_sb[:],
                         start=True, stop=True) for g in range(4)],
                reads=[rTK[1], rID], writes=[rBK[6]])
            S.add("scalar", lambda e: e.activation(
                out=bc[:, 3, :], in_=ps[6][:], func=AF.Copy),
                reads=[rBK[6]], writes=[rBC[3]])
            S.add("vector", lambda e: e.reciprocal(
                out=rcpb[:, 0, :], in_=bc[:, 3, :]), reads=[rBC[3]],
                writes=[rRCP[0]])
            S.add("vector", lambda e: e.tensor_scalar(
                out=rcpb[:, 0, :], in0=rcpb[:, 0, :], scalar1=127.0,
                scalar2=None, op0=ALU.mult), writes=[rRCP[0]])
            S.add("tensor", lambda e: [
                e.matmul(ps[7][:, g * P:(g + 1) * P],
                         tokm.ap()[:, 12 + g:13 + g].broadcast_to([P, P]), id_sb[:],
                         start=True, stop=True) for g in range(4)],
                reads=[rTK[3], rID], writes=[rBK[7]])
            S.add("scalar", lambda e: e.activation(
                out=bc[:, 2, :], in_=ps[7][:], func=AF.Copy),
                reads=[rBK[7]], writes=[rBC[2]])
            S.add("vector", lambda e: e.tensor_mul(
                out=Ff3, in0=Ff3, in1=bsp(rcpb[:, 0, :])), reads=[rRCP[0]], writes=[rF])
            S.add("vector", lambda e: e.tensor_scalar(
                out=sgx3, in0=Ff3, scalar1=MGK, scalar2=MGK,
                op0=ALU.add, op1=ALU.subtract), reads=[rF], writes=[rSG])
            S.add("vector", lambda e: e.memset(haccv[:], 0.0), writes=[rHA],
                  deps=stage_a_done if k == 0 else [])

            for j in range(JC):
                r = j % 4
                S.add("sync", lambda e, j=j, r=r: e.dma_start(
                    out=guv[r], in_=zgu[j]), reads=[rZGU[j]], writes=[rGUV[r]],
                    deps=stage_a_done if (k == 0 and j < 4) else [], dma=True)
                bg, bu = (0, 1) if j % 2 == 0 else (2, 3)
                def fng(e, r=r, bg=bg):
                    return [e.matmul(ps[bg][:], guv[r][:, 0, c, :], sgx3[:, c, :],
                                     start=(c == 0), stop=(c == KC - 1)) for c in range(KC)]
                S.add("tensor", fng, reads=[rGUV[r], rSG], writes=[rBK[bg]])
                def fnu(e, r=r, bu=bu):
                    return [e.matmul(ps[bu][:], guv[r][:, 1, c, :], sgx3[:, c, :],
                                     start=(c == 0), stop=(c == KC - 1)) for c in range(KC)]
                S.add("tensor", fnu, reads=[rGUV[r], rSG], writes=[rBK[bu]])
                S.add("vector", lambda e, bg=bg: e.tensor_mul(
                    out=tmpbf[:, 0, :], in0=ps[bg][:], in1=bc[:, 2, :]),
                    reads=[rBK[bg], rBC[2]], writes=[rTB[0]])
                S.add("scalar", lambda e: e.activation(
                    out=tmpbf[:, 0, :], in_=tmpbf[:, 0, :], func=AF.Silu),
                    writes=[rTB[0]])
                S.add("vector", lambda e, bu=bu: e.tensor_mul(
                    out=tmpbf[:, 1, :], in0=ps[bu][:], in1=bc[:, 2, :]),
                    reads=[rBK[bu], rBC[2]], writes=[rTB[1]])
                S.add("vector", lambda e, j=j: e.tensor_mul(
                    out=hbf[:, j, :], in0=tmpbf[:, 0, :], in1=tmpbf[:, 1, :]),
                    reads=[rTB[0], rTB[1]], writes=[rH[j]],
                    deps=stage_a_done if (k == 0 and j == 0) else [])
                S.add("scalar", lambda e, j=j: e.activation(
                    out=tmpbf[:, 0, :], in_=hbf[:, j, :], func=AF.Square),
                    reads=[rH[j]], writes=[rTB[0]])
                S.add("vector", lambda e: e.tensor_max(
                    out=haccv[:], in0=haccv[:], in1=tmpbf[:, 0, :]),
                    reads=[rTB[0]], writes=[rHA])
            S.add("tensor", lambda e: [
                e.transpose(ps[6][:, g * P:(g + 1) * P], haccv[:, g * P:(g + 1) * P],
                            id_sb[:]) for g in range(4)],
                reads=[rHA, rID], writes=[rBK[6]])
            S.add("vector", lambda e: e.tensor_reduce(
                out=tokm.ap()[:, 16:20].rearrange("p (g o) -> p g o", o=1),
                in_=ps[6][:].rearrange("p (g n) -> p g n", g=4),
                axis=AX.X, op=ALU.max), reads=[rBK[6]], writes=[rTK[4]])
            S.add("scalar", lambda e: e.activation(
                out=tokm.ap()[:, 20:24], in_=tokm.ap()[:, 16:20], func=AF.Sqrt),
                reads=[rTK[4]], writes=[rTK[5]])
            S.add("vector", lambda e: e.tensor_scalar(
                out=tokm.ap()[:, 20:24], in0=tokm.ap()[:, 20:24], scalar1=1e-5,
                scalar2=None, op0=ALU.max), writes=[rTK[5]])
            S.add("vector", lambda e: e.tensor_scalar(
                out=tokm.ap()[:, 28:32], in0=tokm.ap()[:, 20:24], scalar1=1.0 / 127.0,
                scalar2=None, op0=ALU.mult), reads=[rTK[5]], writes=[rTK[7]])
            S.add("tensor", lambda e: [
                e.matmul(ps[6][:, g * P:(g + 1) * P],
                         tokm.ap()[:, 20 + g:21 + g].broadcast_to([P, P]), id_sb[:],
                         start=True, stop=True) for g in range(4)],
                reads=[rTK[5], rID], writes=[rBK[6]])
            S.add("scalar", lambda e: e.activation(
                out=bc[:, 3, :], in_=ps[6][:], func=AF.Copy),
                reads=[rBK[6]], writes=[rBC[3]])
            S.add("vector", lambda e: e.reciprocal(
                out=rcpb[:, 1, :], in_=bc[:, 3, :]), reads=[rBC[3]], writes=[rRCP[1]])
            S.add("vector", lambda e: e.tensor_scalar(
                out=rcpb[:, 1, :], in0=rcpb[:, 1, :], scalar1=127.0,
                scalar2=None, op0=ALU.mult), writes=[rRCP[1]])
            S.add("tensor", lambda e: [
                e.matmul(ps[7][:, g * P:(g + 1) * P],
                         tokm.ap()[:, 28 + g:29 + g].broadcast_to([P, P]), id_sb[:],
                         start=True, stop=True) for g in range(4)],
                reads=[rTK[7], rID], writes=[rBK[7]])
            S.add("scalar", lambda e: e.activation(
                out=bc[:, 3, :], in_=ps[7][:], func=AF.Copy),
                reads=[rBK[7]], writes=[rBC[3]])
            for j in range(JC):
                S.add("vector", lambda e, j=j: e.tensor_mul(
                    out=tmpq[:], in0=hbf[:, j, :], in1=rcpb[:, 1, :]),
                    reads=[rH[j], rRCP[1]], writes=[rTQ])
                S.add("vector", lambda e, j=j: e.tensor_scalar(
                    out=hbf[:, j, :], in0=tmpq[:], scalar1=MGK, scalar2=MGK,
                    op0=ALU.add, op1=ALU.subtract), reads=[rTQ], writes=[rH[j]])
            for j in range(JC):
                r = j % 4
                S.add("sync", lambda e, j=j, r=r: e.dma_start(
                    out=dnv[r], in_=zwd[j]), reads=[rZWD[j]], writes=[rDNV[r]],
                    deps=stage_a_done if (k == 0 and j < 4) else [], dma=True)
                def fnd(e, j=j, r=r):
                    return [e.matmul(ps[m][:], dnv[r][:, m * P:(m + 1) * P],
                                     hbf[:, j, :], start=(j == 0), stop=(j == JC - 1))
                            for m in range(8)]
                S.add("tensor", fnd, reads=[rDNV[r], rH[j]],
                      writes=[rBK[m] for m in range(8)])
            for m in range(8):
                S.add("vector", lambda e, m=m: e.tensor_mul(
                    out=Dd3[:, m, :], in0=ps[m][:], in1=bc[:, 3, :]),
                    reads=[rBK[m], rBC[3]], writes=[rD])
            S.add("scalar", lambda e: e.activation(
                out=sqbf3, in_=Dd3, func=AF.Square), reads=[rD], writes=[rSQ])
            stat_mms(4, sqbf3, extra=[rSQ])
            S.add("scalar", lambda e: e.activation(
                out=rowt[:, 0, :], in_=ps[4][0:1, :], func=AF.Sqrt,
                scale=c_1oD, bias=c_eps), reads=[rBK[4], rCST], writes=[rROW[0]],
                strict=True)
            S.add("vector", lambda e: e.reciprocal(
                out=rowr[1], in_=rowt[:, 0, :]), reads=[rROW[0]], writes=[rRCP[1]])
            S.add("tensor", lambda e: e.matmul(
                ps[6][:], orow_f[:], rowr[1], start=True, stop=True),
                reads=[rRCP[1], rONES], writes=[rBK[6]])
            S.add("scalar", lambda e: e.activation(
                out=bc[:, 0, :], in_=ps[6][:], func=AF.Copy),
                reads=[rBK[6]], writes=[rBC[0]])
            S.add("vector", lambda e: e.tensor_mul(
                out=Dd3, in0=Dd3, in1=bsp(bc[:, 0, :])), reads=[rBC[0]], writes=[rD])
            S.add("gpsimd", lambda e, Xk3=Xk3: e.tensor_add(
                out=Xk3, in0=Xk3, in1=Dd3), reads=[rD], writes=[rXk])
            tk_ox[k] = S.add("sync", lambda e, b=b, t0=t0: e.dma_start(
                out=out_d.rearrange("(c p) t -> p c t", p=P)[:, :, t0:t0 + C],
                in_=Xv3[b]), reads=[rXk], dma=True)

        S.waitonly("sync", tk_ox)
        _emit(nc, S, sems)
    return nc


def _emit(nc, S, sems):
    with nc.Block() as block:
        @block.sync
        def _(eng):
            S.emit("sync", eng, sems)

        @block.tensor
        def _(eng):
            S.emit("tensor", eng, sems)

        @block.vector
        def _(eng):
            S.emit("vector", eng, sems)

        @block.scalar
        def _(eng):
            S.emit("scalar", eng, sems)

        @block.gpsimd
        def _(eng):
            S.emit("gpsimd", eng, sems)


def make_in_map(x_td, Wi, Wf, Wg, conv_i, conv_f, g_norm, Wo, w_gate, w_up, w_down):
    return {
        "x": np.ascontiguousarray(np.asarray(x_td).T),
        "wi": np.ascontiguousarray(Wi), "wf": np.ascontiguousarray(Wf),
        "wg": np.ascontiguousarray(Wg), "wo": np.ascontiguousarray(Wo),
        "ci": np.ascontiguousarray(conv_i), "cf": np.ascontiguousarray(conv_f),
        "gn": np.ascontiguousarray(np.asarray(g_norm).reshape(D, 1)),
        "wgt": np.ascontiguousarray(np.asarray(w_gate).T),
        "wut": np.ascontiguousarray(np.asarray(w_up).T),
        "wdt": np.ascontiguousarray(np.asarray(w_down).T),
        "ident": np.eye(P, dtype=np.float32),
        "consts": np.tile(np.array([[EPS, 1.0 / D, -1.0, 0.0]], np.float32), (P, 1)),
    }


_NC_CACHE = None


def kernel(L0_attn_x, L0_attn_Wi, L0_attn_Wf, L0_attn_Wg, L0_attn_conv_i,
           L0_attn_conv_f, L0_attn_g_norm_weight, L0_attn_Wo,
           L0_ffn_w_gate, L0_ffn_w_up, L0_ffn_w_down):
    global _NC_CACHE
    if _NC_CACHE is None:
        _NC_CACHE = build_nc()
    nc = _NC_CACHE
    x = np.asarray(L0_attn_x)
    in_maps = []
    shared = None
    for b in range(8):
        m = make_in_map(x[b], L0_attn_Wi, L0_attn_Wf, L0_attn_Wg,
                        L0_attn_conv_i, L0_attn_conv_f, L0_attn_g_norm_weight,
                        L0_attn_Wo, L0_ffn_w_gate, L0_ffn_w_up, L0_ffn_w_down)
        if shared is None:
            shared = {k: v for k, v in m.items() if k != "x"}
        else:
            for k in shared:
                m[k] = shared[k]
        in_maps.append(m)
    res = run_bass_kernel_spmd(nc, in_maps, core_ids=list(range(8)))
    out = np.stack([np.asarray(res.results[b]["out"]).T for b in range(8)])
    return np.ascontiguousarray(out).astype(np.float32)
